# version 14
# baseline (speedup 1.0000x reference)
import math
from contextlib import ExitStack

import numpy as np
import ml_dtypes

import concourse.bass as bass
import concourse.mybir as mybir
from concourse import dve_ops as _dve_ops
from concourse.bass_utils import run_bass_kernel_spmd

F32 = mybir.dt.float32
BF16 = mybir.dt.bfloat16
AF = mybir.ActivationFunctionType
ALU = mybir.AluOpType
AX = mybir.AxisListType
BFD = ml_dtypes.bfloat16

FUSED = True

D = 2048
S = 4096
B = 4
NCORE = 8
TOWN = 2048
NG = 48
TEXT = TOWN + NG
NQH = 16
NKV = 8
HD = 128
RANK = 8
LSC = 2.0
INTER = 5632
DIN = 4096
ISC = 1.0 / math.sqrt(HD)
EPS_LN = 1e-5
HC = D // 128
IC0 = INTER // 128
IC1 = DIN // 128
NKVD = NKV * HD
QNW = 11534336


def tblocks(T, step=512):
    out = []
    t = 0
    while t < T:
        w = min(step, T - t)
        out.append((t, w))
        t += w
    return out


TB_OWN = tblocks(TOWN)
TB_EXT = tblocks(TEXT)
TB_KV = tblocks(S)

SCRATCH = {
    "xn_kv": ([D, S], BF16),
    "xn_own": ([D, TEXT], BF16),
    "k_rot": ([NKVD, S], BF16),
    "v_tok": ([S, NKVD], BF16),
    "q_rot": ([D, TEXT], BF16),
    "kmax2": ([1, 1], F32),
    "nbias": ([128, NKV], F32),
    "ao": ([D, TEXT], BF16),
    "x1a": ([D, TEXT], F32),
    "h0": ([INTER, TEXT], BF16),
    "x1": ([D, TEXT], F32),
    "xn1": ([D, TEXT], BF16),
    "xp": ([2 * DIN, TEXT], BF16),
    "h1": ([DIN, TOWN], BF16),
    "x2a": ([D, TOWN], F32),
    "w1qn": ([D, INTER], BF16),
    "w2qn": ([D, INTER], BF16),
    "w3qn": ([INTER, D], BF16),
    "hq": ([INTER, TOWN], BF16),
    "out": ([D, TOWN], F32),
}

STAGED = {
    "x_own": ([D, TEXT], F32),
    "x_kv": ([D, S], F32),
    "wq_t": ([D, D], F32),
    "qA_t": ([D, RANK], F32),
    "qB_t": ([RANK, D], F32),
    "wk_t": ([D, NKVD], F32),
    "kA_t": ([D, RANK], F32),
    "kB_t": ([RANK, NKVD], F32),
    "wv_t": ([D, NKVD], F32),
    "vA_t": ([D, RANK], F32),
    "vB_t": ([RANK, NKVD], F32),
    "wo_t": ([D, D], F32),
    "oA_t": ([D, RANK], F32),
    "oB_t": ([RANK, D], F32),
    "gate_t": ([D, INTER], F32),
    "up_t": ([D, INTER], F32),
    "down_t": ([INTER, D], F32),
    "gate_s": ([128, IC0], F32),
    "up_s": ([128, IC0], F32),
    "down_s": ([128, HC], F32),
    "w_in_t": ([D, 2 * DIN], F32),
    "w_out_t": ([DIN, D], F32),
    "conv_wT": ([128, 4 * IC1], F32),
    "conv_b2": ([128, IC1], F32),
    "Dvec": ([128, IC1], F32),
    "w1_t": ([D, INTER], F32),
    "w2_t": ([D, INTER], F32),
    "w3_t": ([INTER, D], F32),
    "tabq1": ([128, TEXT], F32),
    "tabq2": ([128, TEXT], F32),
    "tabk1": ([128, S], F32),
    "tabk2": ([128, S], F32),
    "maskblk": ([128, 8 * 4 * 512], BF16),
    "gmask": ([128, 32 * 96], BF16),
    "gzero": ([128, NG], BF16),
}

ENGS = ("sync", "gpsimd", "scalar", "vector", "tensor")
NROT = 4


class Ctx:
    def __init__(self, nc, blk, aps, sems, val, phase_idx=0):
        self.nc = nc
        self.blk = blk
        self.t = aps
        self.sem = sems
        self.v = val
        self.phase_idx = phase_idx
        self.gsl_next = 0


class Ph:

    ESEMS = ("tensor", "vector", "scalar", "gpsimd")
    NSLOT = 24

    def __init__(self, ctx):
        self.ctx = ctx
        r = ctx.phase_idx % NROT
        self.esem = {"tensor": f"TE{r}", "vector": f"VE{r}", "scalar": f"SC{r}", "gpsimd": f"GP{r}"}
        self.slots = {}
        self.store_slots = set()
        self.ops = []
        self.cur_group = "main"
        self.gc = {}

    def _slot(self, key, swdge=False):
        if key not in self.slots:
            if swdge:
                name = f"GSL{self.ctx.gsl_next}"
                self.ctx.gsl_next += 1
                assert self.ctx.gsl_next <= NGSL, "GSL pool exhausted"
            else:
                n = sum(1 for k in self.slots if not k.startswith("g:"))
                assert n < self.NSLOT, "slot pool exhausted"
                name = f"SL{n}"
            self.slots[key] = name
        return self.slots[key]

    def _all_counts(self, exclude=None):
        g = self.gc.get(self.cur_group, {})
        return list(g.items())

    def _bump(self, sem):
        self.gc.setdefault(self.cur_group, {})[sem] = self.ctx.v[sem]

    def now(self):
        return self._all_counts()

    def load(self, q, out, in_, waits=(), slot=None):
        sem = self._slot(("g:" if q == "gpsimd" else "") + "L_" + (slot or out.tensor.name),
                         swdge=(q == "gpsimd"))
        self.ctx.v[sem] += 16
        self.ops.append(("dma", q, out, in_, sem, self._mw(waits)))
        return (sem, self.ctx.v[sem])

    def store(self, q, out, in_, waits=(), serial=True, slot=None):
        w = list(waits)
        if serial:
            w += self._all_counts()
        sem = self._slot(("g:" if q == "gpsimd" else "") + "S_" + (slot or in_.tensor.name),
                         swdge=(q == "gpsimd"))
        self.store_slots.add(sem)
        self.ctx.v[sem] += 16
        self._bump(sem)
        self.ops.append(("dma", q, out, in_, sem, self._mw(w)))
        return (sem, self.ctx.v[sem])

    def task(self, eng, fn, waits=(), serial=True):
        w = list(waits)
        if serial:
            w += self._all_counts()
        sem = self.esem[eng]
        self.ctx.v[sem] += 1
        self._bump(sem)
        self.ops.append(("task", eng, fn, None, sem, self._mw(w)))
        return (sem, self.ctx.v[sem])

    def _mw(self, waits):
        d = {}
        for s, v in waits:
            if v > d.get(s, -1):
                d[s] = v
        return sorted(d.items())

    def emit(self):
        ctx = self.ctx
        per = {e: [] for e in ENGS}
        for op in self.ops:
            per[op[1]].append(op)
        for e in ENGS:
            lst = per[e]

            def body(eng, lst=lst):
                seen = {}
                for op in lst:
                    for s, v in op[5]:
                        if v and v > seen.get(s, 0):
                            eng.wait_ge(ctx.sem[s], v)
                            seen[s] = v
                    if op[0] == "dma":
                        eng.dma_start(out=op[2], in_=op[3]).then_inc(ctx.sem[op[4]], 16)
                    else:
                        bi = op[2](eng)
                        assert bi is not None, "task fn must return its last instruction"
                        bi.then_inc(ctx.sem[op[4]], 1)
            getattr(ctx.blk, e)(body)
        release = ctx.v["BAR"] + 5
        drains = [(s, ctx.v[s]) for s in self.esem.values()]
        drains += [(s, ctx.v[s]) for s in self.slots.values()]

        def sync_bar(eng):
            for s, v in drains:
                if v:
                    eng.wait_ge(ctx.sem[s], v)
            eng.nop().then_inc(ctx.sem["BAR"], 1)
            eng.wait_ge(ctx.sem["BAR"], release)
        ctx.blk.sync(sync_bar)
        for e in ("gpsimd", "scalar", "vector", "tensor"):
            def bar(eng):
                eng.drain()
                eng.nop().then_inc(ctx.sem["BAR"], 1)
                eng.wait_ge(ctx.sem["BAR"], release)
            getattr(ctx.blk, e)(bar)
        ctx.v["BAR"] = release
        ctx.phase_idx += 1


NGSL = 48


def sem_names():
    names = ["BAR"]
    for r in range(NROT):
        names += [f"TE{r}", f"VE{r}", f"SC{r}", f"GP{r}"]
    names += [f"SL{k}" for k in range(Ph.NSLOT)]
    names += [f"GSL{k}" for k in range(NGSL)]
    return names


def feat(t):
    return t.rearrange("(c p) t -> p c t", p=128)


def emit_ln(p, ex, ctx, src_name, dst_name, T, tag="ln", lean=False, iter_cb=None,
            xn_share=None, reuse_wait=None):
    nc = ctx.nc
    src, dst = ctx.t[src_name], ctx.t[dst_name]
    tbs = tblocks(T)
    nbuf = 1 if lean else 2
    sb = lambda n, s, d: ex.enter_context(nc.sbuf_tensor(f"z{ctx.phase_idx}_{n}", s, d))
    psum = lambda n, s: ex.enter_context(nc.psum_tensor(f"z{ctx.phase_idx}_{n}", s, F32))
    xs = [sb(f"ln_xs{i}", [128, HC, 512], F32) for i in range(nbuf)]
    xb = [sb(f"ln_xb{i}", [128, HC, 512], BF16) for i in range(nbuf)]
    x2 = None if lean else sb("ln_x2", [128, HC, 512], BF16)
    ones_b = sb("ln_ones_b", [128, 1], BF16)
    ones_f = sb("ln_ones_f", [1, 128], F32)
    rows = sb("ln_rows", [1, 6, 512], F32)
    t1 = None if lean else sb("ln_t1", [128, HC, 512], F32)
    xn = None if xn_share else sb("ln_xn", [128, HC, 512], BF16)
    eps = sb("ln_eps", [1, 1], F32)
    ps_st = psum("ln_ps_st", [128, 512])
    ps1 = ps_st[0:1, :]
    ps2 = ps_st[32:33, :]
    psm = psum("ln_psm", [128, 512])
    psr = psm

    p.task("gpsimd", lambda g: (g.memset(ones_b[:], 1.0), g.memset(ones_f[:], 1.0),
                                g.memset(eps[:], EPS_LN))[-1])
    done = {}
    for i, (t0, w) in enumerate(tbs):
        bi = i % nbuf
        lw = [done[i - nbuf]] if i - nbuf in done else []
        l1 = p.load("sync", xs[bi][:, :, 0:w], feat(src)[:, :, t0:t0 + w], waits=lw)
        l2 = p.load("gpsimd", xb[bi][:, :, 0:w], feat(src)[:, :, t0:t0 + w], waits=lw)

        if lean:
            def stats1(t, bi=bi, w=w):
                for c in range(HC):
                    r = t.matmul(ps1[:, 0:w], ones_b[:], xb[bi][:, c, 0:w],
                                 start=(c == 0), stop=(c == HC - 1), skip_group_check=True)
                return r
            s1t = p.task("tensor", stats1, waits=[l2])

            def sq(v, bi=bi, w=w):
                for c in range(HC):
                    r = v.tensor_tensor(xb[bi][:, c, 0:w], xb[bi][:, c, 0:w], xb[bi][:, c, 0:w],
                                        ALU.mult)
                return r
            sqt = p.task("vector", sq, waits=[s1t])

            def stats2(t, bi=bi, w=w):
                for c in range(HC):
                    r = t.matmul(ps2[:, 0:w], ones_b[:], xb[bi][:, c, 0:w],
                                 start=(c == 0), stop=(c == HC - 1), skip_group_check=True,
                                 tile_position=(0, 32))
                return r
            p.task("tensor", stats2, waits=[sqt])
        else:
            def sq(v, bi=bi, w=w):
                for c in range(HC):
                    r = v.tensor_tensor(x2[:, c, 0:w], xb[bi][:, c, 0:w], xb[bi][:, c, 0:w], ALU.mult)
                return r
            p.task("vector", sq, waits=[l2])

            def stats(t, bi=bi, w=w):
                for c in range(HC):
                    t.matmul(ps1[:, 0:w], ones_b[:], xb[bi][:, c, 0:w],
                             start=(c == 0), stop=(c == HC - 1), skip_group_check=True)
                for c in range(HC):
                    r = t.matmul(ps2[:, 0:w], ones_b[:], x2[:, c, 0:w],
                                 start=(c == 0), stop=(c == HC - 1), skip_group_check=True,
                                 tile_position=(0, 32))
                return r
            p.task("tensor", stats, waits=[l2])

        def rowsab(a, w=w):
            a.activation(rows[:, 0, 0:w], ps1[:, 0:w], AF.Copy, scale=-1.0 / D)
            return a.activation(rows[:, 1, 0:w], ps2[:, 0:w], AF.Copy, scale=1.0 / D)
        p.task("scalar", rowsab)

        p.task("vector", lambda v, w=w: v.tensor_tensor(
            rows[:, 2, 0:w], rows[:, 0, 0:w], rows[:, 0, 0:w], ALU.mult))
        p.task("vector", lambda v, w=w: v.tensor_tensor(
            rows[:, 3, 0:w], rows[:, 1, 0:w], rows[:, 2, 0:w], ALU.subtract))
        p.task("scalar", lambda a, w=w: a.activation(rows[:, 4, 0:w], rows[:, 3, 0:w], AF.Sqrt, bias=eps[:]))
        p.task("vector", lambda v, w=w: v.reciprocal(rows[:, 5, 0:w], rows[:, 4, 0:w]))

        p.task("tensor", lambda t, w=w: t.matmul(
            psm[:, 0:w], ones_f[:], rows[:, 0, 0:w], start=True, stop=True, skip_group_check=True))

        t1b = xs[bi] if lean else t1
        xnt = xn_share[i % 2] if xn_share else xn

        def aff1(v, bi=bi, w=w, t1b=t1b):
            for c in range(HC):
                r = v.tensor_tensor(t1b[:, c, 0:w], xs[bi][:, c, 0:w], psm[:, 0:w], ALU.add)
            return r
        p.task("vector", aff1, waits=[l1])
        p.task("tensor", lambda t, w=w: t.matmul(
            psr[:, 0:w], ones_f[:], rows[:, 5, 0:w], start=True, stop=True, skip_group_check=True))

        rw = []
        if xn_share and reuse_wait is not None:
            h = reuse_wait(i - 2)
            if h is not None:
                rw = [h]

        def aff2(v, w=w, t1b=t1b, xnt=xnt):
            for c in range(HC):
                r = v.tensor_tensor(xnt[:, c, 0:w], t1b[:, c, 0:w], psr[:, 0:w], ALU.mult)
            return r
        a2 = p.task("vector", aff2, waits=rw)
        done[i] = p.store("scalar", feat(dst)[:, :, t0:t0 + w], xnt[:, :, 0:w])
        if iter_cb is not None:
            iter_cb(i, a2)
    return done


def ph_ln(ctx, src_name, dst_name, T):
    p = Ph(ctx)
    with ExitStack() as ex:
        emit_ln(p, ex, ctx, src_name, dst_name, T)
    p.emit()

def emit_kv(p, ex, ctx, ln_done=None, defer=False, xnb_ext=None):
    nc = ctx.nc
    xn = ctx.t["xn_kv"]
    sb = lambda n, s, d: ex.enter_context(nc.sbuf_tensor(f"z{ctx.phase_idx}_{n}", s, d))
    psum = lambda n, s: ex.enter_context(nc.psum_tensor(f"z{ctx.phase_idx}_{n}", s, F32))
    wk = sb("kv_wk", [128, HC, NKVD], BF16)
    wv = sb("kv_wv", [128, HC, NKVD], BF16)
    ka = sb("kv_ka", [128, HC, RANK], BF16)
    kb = sb("kv_kb", [RANK, NKVD], BF16)
    va = sb("kv_va", [128, HC, RANK], BF16)
    vb = sb("kv_vb", [RANK, NKVD], BF16)
    tab1 = [sb(f"kv_tab1{i}", [128, 512], F32) for i in range(2)]
    tab2 = [sb(f"kv_tab2{i}", [128, 512], F32) for i in range(2)]
    xnb = xnb_ext or [sb(f"kv_xnb{i}", [128, HC, 512], BF16) for i in range(2)]
    t1k = sb("kv_t1k", [RANK, 512], BF16)
    t1v = sb("kv_t1v", [RANK, 512], BF16)
    kraw = [sb(f"kv_kraw{i}", [128, 512], F32) for i in range(2)]
    kswap = [sb(f"kv_kswap{i}", [128, 512], F32) for i in range(2)]
    mA = [sb(f"kv_mA{i}", [128, 512], F32) for i in range(2)]
    mB = [sb(f"kv_mB{i}", [128, 512], F32) for i in range(2)]
    krot = sb("kv_krot", [128, NKV, 512], BF16)
    vout = [sb(f"kv_vout{i}", [128, NKVD], BF16) for i in range(2)]
    k2 = [sb(f"kv_k2{i}", [128, 512], BF16) for i in range(2)]
    ones_b = sb("kv_ones_b", [128, 1], BF16)
    kmrow = sb("kv_kmrow", [1, NKV * 8], F32)
    kmfin = sb("kv_kmfin", [1, 1], F32)
    psk = [psum(f"kv_psk{i}", [128, 512]) for i in range(2)]
    psv = [psum(f"kv_psv{i}", [128, 512]) for i in range(2)]
    pst = [psum("kv_pst", [RANK, 512])] * 2
    psq = [psum("kv_psq", [1, 512])] * 2

    lw = [
        p.load("gpsimd", wk[:], feat(ctx.t["wk_t"])),
        p.load("gpsimd", wv[:], feat(ctx.t["wv_t"])),
        p.load("gpsimd", ka[:], feat(ctx.t["kA_t"])),
        p.load("gpsimd", kb[:], ctx.t["kB_t"][:]),
        p.load("gpsimd", va[:], feat(ctx.t["vA_t"])),
        p.load("gpsimd", vb[:], ctx.t["vB_t"][:]),
    ]
    od = p.task("gpsimd", lambda g: g.memset(ones_b[:], 1.0))
    lw.append(od)
    xn_used = {}
    evac_done = {}
    rope_done = {}
    kstore = {}
    psk_free = {}
    add_done = {}
    sq_done = None
    red_done = None
    vevac = {}
    vstore = {}
    sq_free = {}
    psq_free = {}
    t1k_used = None
    t1v_used = None
    pst_free = {}
    def _iter(i, lnh=None):
        nonlocal t1k_used, t1v_used, red_done
        t0, w = TB_KV[i]
        bi = i % 2
        lnw = [lnh] if lnh is not None else []
        xuw = [xn_used[i - 2]] if i - 2 in xn_used else []
        if xnb_ext is None:
            lx = p.load("sync", xnb[bi][:, :, 0:w], feat(xn)[:, :, t0:t0 + w], waits=xuw + lnw)
        else:
            lx = lnh
        lt1 = p.load("sync", tab1[bi][:, 0:w], ctx.t["tabk1"][:, t0:t0 + w], waits=xuw)
        lt2 = p.load("sync", tab2[bi][:, 0:w], ctx.t["tabk2"][:, t0:t0 + w], waits=xuw)

        def lora_k(t, bi=bi, w=w):
            for c in range(HC):
                r = t.matmul(pst[0][:, 0:w], ka[:, c, :], xnb[bi][:, c, 0:w],
                             start=(c == 0), stop=(c == HC - 1), skip_group_check=True)
            return r
        lk = p.task("tensor", lora_k, waits=lw + [lx] + ([pst_free[0]] if 0 in pst_free else []),
                    serial=False)
        ek = p.task("scalar", lambda a, w=w: a.activation(t1k[0:RANK, 0:w], pst[0][0:RANK, 0:w],
                                                          AF.Copy, scale=LSC),
                    waits=[lk] + ([t1k_used] if t1k_used else []), serial=False)
        pst_free[0] = ek

        def lora_v(t, bi=bi, w=w):
            for c in range(HC):
                r = t.matmul(pst[1][:, 0:w], va[:, c, :], xnb[bi][:, c, 0:w],
                             start=(c == 0), stop=(c == HC - 1), skip_group_check=True)
            return r
        lv = p.task("tensor", lora_v, waits=[lx, ek] + ([pst_free[1]] if 1 in pst_free else []),
                    serial=False)
        ev = p.task("scalar", lambda a, w=w: a.activation(t1v[0:RANK, 0:w], pst[1][0:RANK, 0:w],
                                                          AF.Copy, scale=LSC),
                    waits=[lv] + ([t1v_used] if t1v_used else []), serial=False)
        pst_free[1] = ev

        klast = None
        ads = []
        for kv in range(NKV):
            kp = kv % 2

            def kproj(t, bi=bi, w=w, kv=kv, kp=kp):
                for c in range(HC):
                    t.matmul(psk[kp][:, 0:w], wk[:, c, kv * HD:(kv + 1) * HD], xnb[bi][:, c, 0:w],
                             start=(c == 0), stop=False, skip_group_check=True)
                return t.matmul(psk[kp][:, 0:w], kb[:, kv * HD:(kv + 1) * HD], t1k[:, 0:w],
                                start=False, stop=True, skip_group_check=True)
            kj = p.task("tensor", kproj, waits=[ek] + ([psk_free[kp]] if kp in psk_free else []),
                        serial=False)
            t1k_used = kj if kv == NKV - 1 else t1k_used
            ec = p.task("scalar", lambda a, w=w, kp=kp: a.activation(
                kraw[kp][:, 0:w], psk[kp][:, 0:w], AF.Copy),
                waits=[kj] + ([rope_done[kp]] if kp in rope_done else []), serial=False)
            psk_free[kp] = ec
            sw1 = p.load("sync", kswap[kp][0:64, 0:w], kraw[kp][64:128, 0:w], waits=[ec])
            sw2 = p.load("sync", kswap[kp][64:128, 0:w], kraw[kp][0:64, 0:w], waits=[ec])
            ma_t = p.task("vector", lambda v, bi=bi, w=w, kp=kp: v.tensor_tensor(
                mA[kp][:, 0:w], kraw[kp][:, 0:w], tab1[bi][:, 0:w], ALU.mult),
                waits=[ec, lt1] + ([add_done[kp]] if kp in add_done else []), serial=False)
            mb_t = p.task("gpsimd", lambda g, bi=bi, w=w, kp=kp: g.tensor_tensor(
                mB[kp][:, 0:w], kswap[kp][:, 0:w], tab2[bi][:, 0:w], ALU.mult),
                waits=[sw1, sw2, lt2] + ([add_done[kp]] if kp in add_done else []), serial=False)
            ad = p.task("vector", lambda v, kv=kv, w=w, kp=kp: v.tensor_tensor(
                krot[:, kv, 0:w], mA[kp][:, 0:w], mB[kp][:, 0:w], ALU.add),
                waits=[ma_t, mb_t] + ([kstore.get(i - 1)] if i - 1 in kstore and kv == 0 else []),
                serial=False)
            rope_done[kp] = ad
            add_done[kp] = ad
            klast = ad
            ads.append(ad)
        kstore[i] = p.store("sync", feat(ctx.t["k_rot"])[:, :, t0:t0 + w], krot[:, :, 0:w],
                            waits=[klast], serial=False)
        for kv in range(NKV):
            qp = kv % 2
            sq = p.task("scalar", lambda a, kv=kv, w=w, qp=qp: a.activation(
                k2[qp][:, 0:w], krot[:, kv, 0:w], AF.Square),
                waits=[ads[kv]] + ([sq_free[qp]] if qp in sq_free else []), serial=False)
            km = p.task("tensor", lambda t, w=w, qp=qp: t.matmul(
                psq[qp][:, 0:w], ones_b[:], k2[qp][:, 0:w],
                start=True, stop=True, skip_group_check=True),
                waits=[sq] + list(psq_free.values()), serial=False)
            sq_free[qp] = km
            red_done = p.task("vector", lambda v, kv=kv, i=i, w=w, qp=qp: v.reduce_max(
                kmrow[:, kv * 8 + i: kv * 8 + i + 1], psq[qp][:, 0:w], axis=AX.X),
                waits=[km], serial=False)
            psq_free[qp] = red_done
        vlast = None
        for tt in range(w // 128):
            for hh in range(2):
                vp = (tt * 2 + hh) % 2

                def vproj(t, bi=bi, tt=tt, hh=hh, vp=vp):
                    for c in range(HC):
                        t.matmul(psv[vp][:, :], xnb[bi][:, c, tt * 128:(tt + 1) * 128],
                                 wv[:, c, hh * 512:(hh + 1) * 512],
                                 start=(c == 0), stop=False, skip_group_check=True)
                    return t.matmul(psv[vp][:, :], t1v[:, tt * 128:(tt + 1) * 128],
                                    vb[:, hh * 512:(hh + 1) * 512],
                                    start=False, stop=True, skip_group_check=True)
                vj = p.task("tensor", vproj, waits=[ev] + ([vevac[vp]] if vp in vevac else []),
                            serial=False)
                if tt == w // 128 - 1 and hh == 1:
                    t1v_used = vj
                vo = tt % 2
                vevac[vp] = p.task("scalar", lambda a, hh=hh, vp=vp, vo=vo: a.activation(
                    vout[vo][:, hh * 512:(hh + 1) * 512], psv[vp][:, :], AF.Copy),
                    waits=[vj] + ([vstore.get(vo)] if vo in vstore and hh == 0 else []), serial=False)
            if True:
                vstore[tt % 2] = p.store("sync",
                                         ctx.t["v_tok"][t0 + tt * 128: t0 + (tt + 1) * 128, :],
                                         vout[tt % 2][:, 0:NKVD],
                                         waits=[vevac[0], vevac[1]] if 0 in vevac and 1 in vevac
                                         else list(vevac.values()), serial=False)
                vlast = vstore[tt % 2]
        xn_used[i] = klast if vlast is None else vlast

    def _fin():
        fr = p.task("vector", lambda v: v.reduce_max(kmfin[:], kmrow[:], axis=AX.X),
                    waits=[red_done], serial=True)
        p.store("sync", ctx.t["kmax2"][:], kmfin[:], waits=[fr], serial=False)



    if not defer:
        for _i in range(len(TB_KV)):
            _iter(_i, ln_done[_i] if ln_done else None)
        _fin()
    return _iter, _fin, xn_used

def ph_kv(ctx):
    p = Ph(ctx)
    with ExitStack() as ex:
        r = emit_kv(p, ex, ctx)
    p.emit()

def emit_q(p, ex, ctx, ln_done=None, defer=False, xnb_ext=None):
    nc = ctx.nc
    xn = ctx.t["xn_own"]
    sb = lambda n, s, d: ex.enter_context(nc.sbuf_tensor(f"z{ctx.phase_idx}_{n}", s, d))
    psum = lambda n, s: ex.enter_context(nc.psum_tensor(f"z{ctx.phase_idx}_{n}", s, F32))
    wq = sb("q_wq", [128, HC, D], BF16)
    qa = sb("q_qa", [128, HC, RANK], BF16)
    qb = sb("q_qb", [RANK, D], BF16)
    tab1 = [sb(f"q_tab1{i}", [128, 512], F32) for i in range(2)]
    tab2 = [sb(f"q_tab2{i}", [128, 512], F32) for i in range(2)]
    xnb = xnb_ext or [sb(f"q_xnb{i}", [128, HC, 512], BF16) for i in range(2)]
    t1q = sb("q_t1q", [RANK, 512], BF16)
    qraw = [sb(f"q_qraw{i}", [128, 512], F32) for i in range(2)]
    qswap = [sb(f"q_qswap{i}", [128, 512], F32) for i in range(2)]
    mA = [sb(f"q_mA{i}", [128, 512], F32) for i in range(2)]
    mB = [sb(f"q_mB{i}", [128, 512], F32) for i in range(2)]
    qrot = sb("q_qrot", [128, NQH, 512], BF16)
    q2 = [sb(f"q_q2{i}", [128, 512], BF16) for i in range(2)]
    ones_b = sb("q_ones_b", [128, 1], BF16)
    ones_f = sb("q_ones_f", [1, 128], F32)
    qmrow = sb("q_qmrow", [1, NQH * 5], F32)
    km = sb("q_km", [1, 1], F32)
    sc1 = sb("q_sc1", [1, NKV], F32)
    nb = sb("q_nb", [128, NKV], F32)
    psq = [psum(f"q_psq{i}", [128, 512]) for i in range(2)]
    pst = psum("q_pst", [RANK, 512])
    ps2 = [psum("q_ps2", [1, 512])] * 2
    psb = psum("q_psb", [128, 1])

    lw = [
        p.load("gpsimd", wq[:], feat(ctx.t["wq_t"])),
        p.load("gpsimd", qa[:], feat(ctx.t["qA_t"])),
        p.load("gpsimd", qb[:], ctx.t["qB_t"][:]),
        p.load("sync", km[:], ctx.t["kmax2"][:]),
    ]
    od = p.task("gpsimd", lambda g: (g.memset(ones_b[:], 1.0), g.memset(ones_f[:], 1.0))[-1])
    lw.append(od)
    xn_used = {}
    evac_done = {}
    rope_done = {}
    add_done = {}
    qstore = {}
    psq_free = {}
    sq_free = {}
    ps2_free = {}
    red_done = None
    t1_used = None
    pst_free = None
    def _iter(i, lnh=None):
        nonlocal t1_used, pst_free, red_done
        t0, w = TB_EXT[i]
        bi = i % 2
        lnw = [lnh] if lnh is not None else []
        xuw = [xn_used[i - 2]] if i - 2 in xn_used else []
        if xnb_ext is None:
            lx = p.load("sync", xnb[bi][:, :, 0:w], feat(xn)[:, :, t0:t0 + w], waits=xuw + lnw)
        else:
            lx = lnh
        lt1 = p.load("sync", tab1[bi][:, 0:w], ctx.t["tabq1"][:, t0:t0 + w], waits=xuw)
        lt2 = p.load("sync", tab2[bi][:, 0:w], ctx.t["tabq2"][:, t0:t0 + w], waits=xuw)

        def lora(t, bi=bi, w=w):
            for c in range(HC):
                r = t.matmul(pst[:, 0:w], qa[:, c, :], xnb[bi][:, c, 0:w],
                             start=(c == 0), stop=(c == HC - 1), skip_group_check=True)
            return r
        lq = p.task("tensor", lora, waits=lw + [lx] + ([pst_free] if pst_free else []), serial=False)
        et = p.task("scalar", lambda a, w=w: a.activation(t1q[0:RANK, 0:w], pst[0:RANK, 0:w],
                                                          AF.Copy, scale=LSC),
                    waits=[lq] + ([t1_used] if t1_used else []), serial=False)
        pst_free = et

        qlast = None
        ads = []
        for h in range(NQH):
            kp = h % 2

            def qproj(t, bi=bi, w=w, h=h, kp=kp):
                for c in range(HC):
                    t.matmul(psq[kp][:, 0:w], wq[:, c, h * HD:(h + 1) * HD], xnb[bi][:, c, 0:w],
                             start=(c == 0), stop=False, skip_group_check=True)
                return t.matmul(psq[kp][:, 0:w], qb[:, h * HD:(h + 1) * HD], t1q[:, 0:w],
                                start=False, stop=True, skip_group_check=True)
            qj = p.task("tensor", qproj, waits=[et] + ([psq_free[kp]] if kp in psq_free else []),
                        serial=False)
            if h == NQH - 1:
                t1_used = qj
            ec = p.task("scalar", lambda a, w=w, kp=kp: a.activation(
                qraw[kp][:, 0:w], psq[kp][:, 0:w], AF.Copy),
                waits=[qj] + ([rope_done[kp]] if kp in rope_done else []), serial=False)
            psq_free[kp] = ec
            sw1 = p.load("sync", qswap[kp][0:64, 0:w], qraw[kp][64:128, 0:w], waits=[ec])
            sw2 = p.load("sync", qswap[kp][64:128, 0:w], qraw[kp][0:64, 0:w], waits=[ec])
            ma_t = p.task("vector", lambda v, bi=bi, w=w, kp=kp: v.tensor_tensor(
                mA[kp][:, 0:w], qraw[kp][:, 0:w], tab1[bi][:, 0:w], ALU.mult),
                waits=[ec, lt1] + ([add_done[kp]] if kp in add_done else []), serial=False)
            mb_t = p.task("gpsimd", lambda g, bi=bi, w=w, kp=kp: g.tensor_tensor(
                mB[kp][:, 0:w], qswap[kp][:, 0:w], tab2[bi][:, 0:w], ALU.mult),
                waits=[sw1, sw2, lt2] + ([add_done[kp]] if kp in add_done else []), serial=False)
            ad = p.task("vector", lambda v, h=h, w=w, kp=kp: v.tensor_tensor(
                qrot[:, h, 0:w], mA[kp][:, 0:w], mB[kp][:, 0:w], ALU.add),
                waits=[ma_t, mb_t] + ([qstore.get(i - 1)] if i - 1 in qstore and h == 0 else []),
                serial=False)
            rope_done[kp] = ad
            add_done[kp] = ad
            qlast = ad
            ads.append(ad)
        qstore[i] = p.store("sync", feat(ctx.t["q_rot"])[:, :, t0:t0 + w], qrot[:, :, 0:w],
                            waits=[qlast], serial=False)
        xn_used[i] = qlast
        for h in range(NQH):
            qp2 = h % 2
            sq = p.task("scalar", lambda a, h=h, w=w, qp2=qp2: a.activation(
                q2[qp2][:, 0:w], qrot[:, h, 0:w], AF.Square),
                waits=[ads[h]] + ([sq_free[qp2]] if qp2 in sq_free else []), serial=False)
            kmm = p.task("tensor", lambda t, w=w, qp2=qp2: t.matmul(
                ps2[qp2][:, 0:w], ones_b[:], q2[qp2][:, 0:w],
                start=True, stop=True, skip_group_check=True),
                waits=[sq] + list(ps2_free.values()), serial=False)
            sq_free[qp2] = kmm
            red_done = p.task("vector", lambda v, h=h, i=i, w=w, qp2=qp2: v.reduce_max(
                qmrow[:, h * 5 + i: h * 5 + i + 1], ps2[qp2][:, 0:w], axis=AX.X),
                waits=[kmm], serial=False)
            ps2_free[qp2] = red_done


    def _fin():
        for kv in range(NKV):
            p.task("vector", lambda v, kv=kv: v.reduce_max(
                sc1[:, kv:kv + 1], qmrow[:, 2 * kv * 5:(2 * kv + 2) * 5], axis=AX.X))
        p.task("vector", lambda v: v.tensor_scalar_mul(sc1[:], sc1[:], km[0:1, 0:1]))
        p.task("scalar", lambda a: a.activation(sc1[:], sc1[:], AF.Sqrt))
        p.task("scalar", lambda a: a.mul(sc1[:], sc1[:], -ISC))
        for kv in range(NKV):
            p.task("tensor", lambda t, kv=kv: t.matmul(psb[:], ones_f[:], sc1[:, kv:kv + 1],
                                                       start=True, stop=True, skip_group_check=True))
            p.task("scalar", lambda a, kv=kv: a.activation(nb[:, kv:kv + 1], psb[:], AF.Copy))
        p.store("sync", ctx.t["nbias"][:], nb[:])



    if not defer:
        for _i in range(len(TB_EXT)):
            _iter(_i, ln_done[_i] if ln_done else None)
        _fin()
    return _iter, _fin, xn_used

def ph_q(ctx):
    p = Ph(ctx)
    with ExitStack() as ex:
        r = emit_q(p, ex, ctx)
    p.emit()

def emit_attn(p, ex, ctx, feeder=None):
    nc = ctx.nc
    sb = lambda n, s, d: ex.enter_context(nc.sbuf_tensor(f"z{ctx.phase_idx}_{n}", s, d))
    ksb = [sb(f"at_k{i}", [128, S], BF16) for i in range(2)]
    vsb = [sb(f"at_v{i}", [128, 32, 128], BF16) for i in range(2)]
    qsb = [sb(f"at_q{i}", [128, 2, TEXT], BF16) for i in range(2)]
    mask = sb("at_mask", [128, 8 * 4 * 512], BF16)
    gmask = sb("at_gmask", [128, 32 * 96], BF16)
    nb = sb("at_nb", [128, NKV], F32)
    ones_b = sb("at_ones", [128, 128], BF16)
    NPB = 5
    probs = [sb(f"at_pb{i}", [128, 512], BF16) for i in range(NPB)]
    rl = sb("at_rl", [128, 512], F32)
    rscr = sb("at_rscr", [128, 512], F32)
    aos = [sb(f"at_ao{i}", [128, 512], BF16) for i in range(2)]
    pss = ex.enter_context(nc.psum_tensor(f"z{ctx.phase_idx}_at_pss", [128, 3, 512], F32))
    psav = [ex.enter_context(nc.psum_tensor(f"z{ctx.phase_idx}_at_psav{i}", [128, 512], F32)) for i in range(2)]
    psl = [ex.enter_context(nc.psum_tensor(f"z{ctx.phase_idx}_at_psl{i}", [128, 512], F32)) for i in range(2)]

    base = [
        p.load("sync", mask[:], ctx.t["maskblk"][:]),
        p.load("sync", gmask[:], ctx.t["gmask"][:]),
        p.load("sync", nb[:], ctx.t["nbias"][:]),
    ]
    onesdone = p.task("gpsimd", lambda g: g.memset(ones_b[:], 1.0))

    kvload = {}
    kv_consumed = {}

    def load_kv(kv):
        bi = kv % 2
        w = [kv_consumed[kv - 2]] if kv - 2 in kv_consumed else []
        k_l = p.load("sync", ksb[bi][:], ctx.t["k_rot"][kv * 128:(kv + 1) * 128, :], waits=w)
        v_l = p.load("sync", vsb[bi][:],
                     ctx.t["v_tok"][:, kv * 128:(kv + 1) * 128].rearrange("(kt p) d -> p kt d", p=128),
                     waits=w)
        q_l = p.load("sync", qsb[bi][:],
                     feat(ctx.t["q_rot"])[:, 2 * kv:2 * kv + 2, :], waits=w)
        kvload[kv] = [k_l, v_l, q_l]

    load_kv(0)
    exp_done = {}
    probs_done = {}
    epi = {}
    stores = {}
    gqb = 0
    mask_flip = 0

    for kv in range(NKV):
        if kv + 1 < NKV:
            load_kv(kv + 1)
        bi = kv % 2
        lastav = None
        for qb in range(9):
            ghost = (qb == 8)
            nq = 96 if ghost else 512
            q0 = TOWN if ghost else qb * 256
            qw = NG if ghost else 256
            nkt = 32 if ghost else 4 * qb + 4
            pv = gqb % 2
            av_started = False
            pend = []

            def emit_av(kt, pbuf, ready, last):
                nonlocal av_started, lastav
                aw = [ready]
                if not av_started and gqb - 2 in epi:
                    aw.append(epi[gqb - 2])

                def mm_av(t, bi=bi, pv=pv, pbuf=pbuf, kt=kt, nq=nq,
                          st=not av_started, sp=last):
                    t.matmul(psav[pv][:, 0:nq], vsb[bi][:, kt, :], probs[pbuf][:, 0:nq],
                             start=st, stop=sp, skip_group_check=True)
                    return t.matmul(psl[pv][:, 0:nq], ones_b[:, :], probs[pbuf][:, 0:nq],
                                    start=st, stop=sp, skip_group_check=True)
                av_t = p.task("tensor", mm_av, waits=aw, serial=False)
                av_started = True
                probs_done[pbuf] = av_t
                lastav = av_t

            for kt in range(nkt):
                sbank = kt % 3
                pbuf = kt % NPB
                masked = ghost or (kt >= 4 * qb)
                sw = list(base) + kvload[kv] + [onesdone]
                if sbank in exp_done:
                    sw.append(exp_done[sbank])

                def mm_s(t, bi=bi, sbank=sbank, kt=kt, q0=q0, qw=qw, nq=nq):
                    return t.matmul(
                        pss[:, sbank, 0:nq].rearrange("k (h q) -> k h q", h=2),
                        ksb[bi][:, kt * 128:(kt + 1) * 128],
                        qsb[bi][:, :, q0:q0 + qw],
                        start=True, stop=True, skip_group_check=True)
                sc_t = p.task("tensor", mm_s, waits=sw, serial=False)

                ew = [sc_t]
                if pbuf in probs_done:
                    ew.append(probs_done[pbuf])

                def ex_f(a, sbank=sbank, pbuf=pbuf, kv=kv, nq=nq):
                    return a.activation(probs[pbuf][:, 0:nq], pss[:, sbank, 0:nq], AF.Exp,
                                        bias=nb[:, kv:kv + 1], scale=ISC)
                ex_t = p.task("scalar", ex_f, waits=ew, serial=False)
                exp_done[sbank] = ex_t

                if masked:
                    if ghost:
                        moff, msrc, mw_ = kt * 96, gmask, 96
                    else:
                        moff, msrc, mw_ = (qb * 4 + (kt - 4 * qb)) * 512, mask, 512
                    meng = "gpsimd"
                    mask_flip += 1

                    def mk_f(v, pbuf=pbuf, moff=moff, msrc=msrc, mw_=mw_):
                        return v.tensor_tensor(probs[pbuf][:, 0:mw_], probs[pbuf][:, 0:mw_],
                                               msrc[:, moff:moff + mw_], ALU.mult)
                    ready = p.task(meng, mk_f, waits=[ex_t], serial=False)
                else:
                    ready = ex_t
                pend.append((kt, pbuf, ready))
                if len(pend) > 2:
                    k_, pb_, rd_ = pend.pop(0)
                    emit_av(k_, pb_, rd_, k_ == nkt - 1)
            while pend:
                k_, pb_, rd_ = pend.pop(0)
                emit_av(k_, pb_, rd_, k_ == nkt - 1)
            rc_t = p.task("vector", lambda v, pv=pv, nq=nq: v.reciprocal(
                rl[:, 0:nq], psl[pv][:, 0:nq]),
                waits=[lastav] + ([epi[gqb - 1]] if gqb - 1 in epi else []), serial=False)
            cw = [rc_t] + ([stores[gqb - 2]] if gqb - 2 in stores else [])
            dv_t = p.task("vector", lambda v, pv=pv, nq=nq, ab=gqb % 2: v.tensor_tensor(
                aos[ab][:, 0:nq], psav[pv][:, 0:nq], rl[:, 0:nq], ALU.mult),
                waits=cw, serial=False)
            epi[gqb] = dv_t
            ao = ctx.t["ao"]
            h0r = feat(ao)[:, 2 * kv, q0:q0 + qw]
            h1r = feat(ao)[:, 2 * kv + 1, q0:q0 + qw]
            ab = gqb % 2
            p.store("gpsimd", h0r, aos[ab][:, 0:qw], waits=[dv_t], serial=False)
            stores[gqb] = p.store("gpsimd", h1r, aos[ab][:, qw:2 * qw], waits=[dv_t], serial=False)
            gqb += 1
            if feeder is not None:
                feeder()
        kv_consumed[kv] = lastav


def ph_attn(ctx):
    p = Ph(ctx)
    with ExitStack() as ex:
        r = emit_attn(p, ex, ctx)
    p.emit()

def ph_oproj(ctx):
    nc = ctx.nc
    p = Ph(ctx)
    with ExitStack() as ex:
        sb = lambda n, s, d: ex.enter_context(nc.sbuf_tensor(f"z{ctx.phase_idx}_{n}", s, d))
        psum = lambda n, s: ex.enter_context(nc.psum_tensor(f"z{ctx.phase_idx}_{n}", s, F32))
        wo = sb("op_wo", [128, HC, D], BF16)
        oa = sb("op_oa", [128, HC, RANK], BF16)
        ob = sb("op_ob", [RANK, D], BF16)
        aob = [sb(f"op_ao{i}", [128, HC, 512], BF16) for i in range(2)]
        xs = [sb(f"op_xs{i}", [128, HC, 512], F32) for i in range(2)]
        t1 = sb("op_t1", [RANK, 512], BF16)
        xout = sb("op_xout", [128, HC, 512], F32)
        pso = [psum(f"op_pso{i}", [128, 512]) for i in range(2)]
        pst = psum("op_pst", [RANK, 512])

        lw = [
            p.load("gpsimd", wo[:], feat(ctx.t["wo_t"])),
            p.load("gpsimd", oa[:], feat(ctx.t["oA_t"])),
            p.load("gpsimd", ob[:], ctx.t["oB_t"][:]),
        ]
        done = {}
        used = {}
        evac_done = {}
        t1_used = None
        pst_free = None
        for i, (t0, w) in enumerate(TB_EXT):
            bi = i % 2
            ws = [used[i - 2]] if i - 2 in used else []
            la = p.load("sync", aob[bi][:, :, 0:w], feat(ctx.t["ao"])[:, :, t0:t0 + w], waits=ws)
            lx = p.load("sync", xs[bi][:, :, 0:w], feat(ctx.t["x_own"])[:, :, t0:t0 + w], waits=ws)

            def lora(t, bi=bi, w=w):
                for c in range(HC):
                    r = t.matmul(pst[:, 0:w], oa[:, c, :], aob[bi][:, c, 0:w],
                                 start=(c == 0), stop=(c == HC - 1), skip_group_check=True)
                return r
            lo = p.task("tensor", lora, waits=lw + [la] + ([pst_free] if pst_free else []), serial=False)
            et = p.task("scalar", lambda a, w=w: a.activation(t1[0:RANK, 0:w], pst[0:RANK, 0:w],
                                                              AF.Copy, scale=LSC),
                        waits=[lo] + ([t1_used] if t1_used else []), serial=False)
            pst_free = et
            evs = []
            for oc in range(HC):
                pb = oc % 2

                def proj(t, bi=bi, w=w, oc=oc, pb=pb):
                    for c in range(HC):
                        t.matmul(pso[pb][:, 0:w], wo[:, c, oc * HD:(oc + 1) * HD], aob[bi][:, c, 0:w],
                                 start=(c == 0), stop=False, skip_group_check=True)
                    return t.matmul(pso[pb][:, 0:w], ob[:, oc * HD:(oc + 1) * HD], t1[:, 0:w],
                                    start=False, stop=True, skip_group_check=True)
                pj = p.task("tensor", proj, waits=[et] + ([evac_done[pb]] if pb in evac_done else []),
                            serial=False)
                if oc == HC - 1:
                    t1_used = pj
                ew = [pj, lx] + ([done[i - 1]] if i - 1 in done and oc == 0 else [])
                ev = p.task("vector", lambda v, bi=bi, w=w, oc=oc, pb=pb: v.tensor_tensor(
                    xout[:, oc, 0:w], pso[pb][:, 0:w], xs[bi][:, oc, 0:w], ALU.add),
                    waits=ew, serial=False)
                evac_done[pb] = ev
                evs.append(ev)
            used[i] = evs[-1]
            done[i] = p.store("sync", feat(ctx.t["x1a"])[:, :, t0:t0 + w], xout[:, :, 0:w],
                              waits=evs, serial=False)
    p.emit()


def ph_ffn0a(ctx):
    nc = ctx.nc
    p = Ph(ctx)
    with ExitStack() as ex:
        sb = lambda n, s, d: ex.enter_context(nc.sbuf_tensor(f"z{ctx.phase_idx}_{n}", s, d))
        psum = lambda n, s: ex.enter_context(nc.psum_tensor(f"z{ctx.phase_idx}_{n}", s, F32))
        xn = sb("f0_xn", [128, HC, TEXT], BF16)
        gw = [sb(f"f0_gw{i}", [128, HC, 128], BF16) for i in range(2)]
        uw = [sb(f"f0_uw{i}", [128, HC, 128], BF16) for i in range(2)]
        gs = sb("f0_gs", [128, IC0], F32)
        us = sb("f0_us", [128, IC0], F32)
        sg = [sb(f"f0_sg{i}", [128, 512], BF16) for i in range(2)]
        hsb = [sb(f"f0_h{i}", [128, TEXT], BF16) for i in range(2)]
        psg = [psum(f"f0_psg{i}", [128, 512]) for i in range(2)]
        psu = [psum(f"f0_psu{i}", [128, 512]) for i in range(2)]

        lw = [
            p.load("sync", xn[:], feat(ctx.t["xn_own"])),
            p.load("sync", gs[:], ctx.t["gate_s"][:]),
            p.load("sync", us[:], ctx.t["up_s"][:]),
        ]
        done = {}
        wconsumed = {}
        silu_done = {}
        stt_done = {}
        for ic in range(IC0):
            bi = ic % 2
            ws = [wconsumed[ic - 2]] if ic - 2 in wconsumed else []
            lg = p.load("gpsimd", gw[bi][:], feat(ctx.t["gate_t"])[:, :, ic * 128:(ic + 1) * 128], waits=ws)
            lu = p.load("gpsimd", uw[bi][:], feat(ctx.t["up_t"])[:, :, ic * 128:(ic + 1) * 128], waits=ws)
            stts = []
            for tbi, (t0, w) in enumerate(TB_EXT):
                pb = tbi % 2

                def mmg(t, bi=bi, t0=t0, w=w, pb=pb):
                    for c in range(HC):
                        r = t.matmul(psg[pb][:, 0:w], gw[bi][:, c, :], xn[:, c, t0:t0 + w],
                                     start=(c == 0), stop=(c == HC - 1), skip_group_check=True)
                    return r
                gw_ = lw + [lg] + ([silu_done[pb]] if pb in silu_done else [])
                mg_t = p.task("tensor", mmg, waits=gw_, serial=False)

                sw_ = [mg_t] + ([stt_done[pb]] if pb in stt_done else [])
                sl_t = p.task("scalar", lambda a, ic=ic, w=w, pb=pb: a.activation(
                    sg[pb][:, 0:w], psg[pb][:, 0:w], AF.Silu, scale=gs[:, ic:ic + 1]),
                    waits=sw_, serial=False)
                silu_done[pb] = sl_t

                def mmu(t, bi=bi, t0=t0, w=w, pb=pb):
                    for c in range(HC):
                        r = t.matmul(psu[pb][:, 0:w], uw[bi][:, c, :], xn[:, c, t0:t0 + w],
                                     start=(c == 0), stop=(c == HC - 1), skip_group_check=True)
                    return r
                uw_ = [lu] + ([stt_done[pb]] if pb in stt_done else [])
                mu_t = p.task("tensor", mmu, waits=uw_, serial=False)

                tw_ = [mu_t, sl_t] + ([done[ic - 2]] if ic - 2 in done and tbi == 0 else [])
                st_t = p.task("vector", lambda v, ic=ic, bi=bi, t0=t0, w=w, pb=pb: v.scalar_tensor_tensor(
                    hsb[bi][:, t0:t0 + w], psu[pb][:, 0:w], us[:, ic:ic + 1], sg[pb][:, 0:w],
                    ALU.mult, ALU.mult), waits=tw_, serial=False)
                stt_done[pb] = st_t
                stts.append(st_t)
            wconsumed[ic] = stts[-1]
            done[ic] = p.store("scalar", ctx.t["h0"][ic * 128:(ic + 1) * 128, :], hsb[bi][:],
                               waits=stts, serial=False)
    p.emit()


def ph_ffn0b(ctx):
    nc = ctx.nc
    p = Ph(ctx)
    superblocks = [(0, 1024), (1024, TEXT - 1024)]
    with ExitStack() as ex:
        sb = lambda n, s, d: ex.enter_context(nc.sbuf_tensor(f"z{ctx.phase_idx}_{n}", s, d))
        psum = lambda n, s: ex.enter_context(nc.psum_tensor(f"z{ctx.phase_idx}_{n}", s, F32))
        hb = sb("f0b_h", [128, IC0, 1072], BF16)
        dw = [sb(f"f0b_dw{i}", [128, IC0, 128], BF16) for i in range(2)]
        ds = sb("f0b_ds", [128, HC], F32)
        xa = [sb(f"f0b_xa{i}", [128, 512], F32) for i in range(2)]
        xo = [sb(f"f0b_xo{i}", [128, 512], F32) for i in range(2)]
        pso = [psum(f"f0b_pso{i}", [128, 512]) for i in range(2)]

        lds = p.load("sync", ds[:], ctx.t["down_s"][:])
        h_consumed = None
        wconsumed = {}
        evac_done = {}
        xa_done = {}
        st_done = {}
        gi = 0
        for si, (s0, sw) in enumerate(superblocks):
            lh = p.load("sync", hb[:, :, 0:sw], feat(ctx.t["h0"])[:, :, s0:s0 + sw],
                        waits=([h_consumed] if h_consumed else []))
            for oc in range(HC):
                bi = oc % 2
                gw = si * HC + oc
                ldw = p.load("gpsimd", dw[bi][:], feat(ctx.t["down_t"])[:, :, oc * 128:(oc + 1) * 128],
                             waits=([wconsumed[gw - 2]] if gw - 2 in wconsumed else []))
                for (t0, w) in tblocks(sw):
                    pb = gi % 2
                    lx = p.load("sync", xa[pb][:, 0:w], feat(ctx.t["x1a"])[:, oc, s0 + t0:s0 + t0 + w],
                                waits=([xa_done[pb]] if pb in xa_done else []))

                    def mm(t, bi=bi, t0=t0, w=w, pb=pb):
                        for c in range(IC0):
                            r = t.matmul(pso[pb][:, 0:w], dw[bi][:, c, :], hb[:, c, t0:t0 + w],
                                         start=(c == 0), stop=(c == IC0 - 1), skip_group_check=True)
                        return r
                    mm_t = p.task("tensor", mm, waits=[lh, ldw] + ([evac_done[pb]] if pb in evac_done else []),
                                  serial=False)
                    ev_t = p.task("vector", lambda v, oc=oc, w=w, pb=pb: v.scalar_tensor_tensor(
                        xo[pb][:, 0:w], pso[pb][:, 0:w], ds[:, oc:oc + 1], xa[pb][:, 0:w], ALU.mult, ALU.add),
                        waits=[mm_t, lx, lds] + ([st_done[pb]] if pb in st_done else []), serial=False)
                    evac_done[pb] = ev_t
                    xa_done[pb] = ev_t
                    st_done[pb] = p.store("sync", feat(ctx.t["x1"])[:, oc, s0 + t0:s0 + t0 + w],
                                          xo[pb][:, 0:w], waits=[ev_t], serial=False)
                    gi += 1
                wconsumed[gw] = mm_t
            h_consumed = mm_t
    p.emit()


def emit_win(p, ex, ctx, ln_done=None, pair_cb=None):
    nc = ctx.nc
    sb = lambda n, s, d: ex.enter_context(nc.sbuf_tensor(f"z{ctx.phase_idx}_{n}", s, d))
    psum = lambda n, s: ex.enter_context(nc.psum_tensor(f"z{ctx.phase_idx}_{n}", s, F32))
    xn = sb("wi_xn", [128, HC, TEXT], BF16)
    ww = [sb(f"wi_w{i}", [128, HC, 128], BF16) for i in range(2)]
    xo = [sb(f"wi_xo{i}", [128, TEXT], BF16) for i in range(2)]
    pso = [psum(f"wi_pso{i}", [128, 512]) for i in range(2)]

    lxn = {}
    for ti, (t0, w) in enumerate(TB_EXT):
        lxn[ti] = p.load("sync", xn[:, :, t0:t0 + w], feat(ctx.t["xn1"])[:, :, t0:t0 + w],
                         waits=([ln_done[ti]] if ln_done else []), slot=f"wi_xn{ti}")
    done = {}
    wconsumed = {}
    evac_done = {}
    oc_order = [c for pair in zip(range(IC1), range(IC1, 2 * IC1)) for c in pair]
    pair_ready = {}
    for oi, oc in enumerate(oc_order):
        bi = oi % 2
        ws = [wconsumed[oi - 2]] if oi - 2 in wconsumed else []
        lw = p.load("gpsimd", ww[bi][:], feat(ctx.t["w_in_t"])[:, :, oc * 128:(oc + 1) * 128], waits=ws)
        evs = []
        for ti, (t0, w) in enumerate(TB_EXT):
            pb = ti % 2

            def mm(t, bi=bi, t0=t0, w=w, pb=pb):
                for c in range(HC):
                    r = t.matmul(pso[pb][:, 0:w], ww[bi][:, c, :], xn[:, c, t0:t0 + w],
                                 start=(c == 0), stop=(c == HC - 1), skip_group_check=True)
                return r
            mw_ = [lxn[ti], lw] + ([evac_done[pb]] if pb in evac_done else [])
            mm_t = p.task("tensor", mm, waits=mw_, serial=False)
            ew_ = [mm_t] + ([done[oc_order[oi - 2]]] if oi - 2 >= 0 and ti == 0 else [])
            ev_t = p.task("scalar", lambda a, bi=bi, t0=t0, w=w, pb=pb: a.activation(
                xo[bi][:, t0:t0 + w], pso[pb][:, 0:w], AF.Copy), waits=ew_, serial=False)
            evac_done[pb] = ev_t
            evs.append(ev_t)
        wconsumed[oi] = mm_t
        done[oc] = p.store("sync", ctx.t["xp"][oc * 128:(oc + 1) * 128, :], xo[bi][:],
                           waits=evs, serial=False)
        if pair_cb is not None and oi % 2 == 1:
            pair_cb(oc_order[oi - 1], done[oc_order[oi - 1]], done[oc])
    return done


def ph_win(ctx):
    p = Ph(ctx)
    with ExitStack() as ex:
        r = emit_win(p, ex, ctx)
    p.emit()

def build_conv(p, ex, ctx):
    nc = ctx.nc
    sb = lambda n, s, d: ex.enter_context(nc.sbuf_tensor(f"z{ctx.phase_idx}_{n}", s, d))
    cwt = [sb(f"cv_cw{t}", [128, IC1], F32) for t in range(4)]
    cb = sb("cv_cb", [128, IC1], F32)
    dv = sb("cv_dv", [128, IC1], F32)
    gz = sb("cv_gz", [128, NG], BF16)
    xi = [sb(f"cv_xi{i}", [128, TEXT], BF16) for i in range(2)]
    gt = sb("cv_gt", [128, TOWN], BF16)
    asm = sb("cv_asm", [128, 16 * 131], BF16)
    ac = [sb(f"cv_ac{i}", [128, TOWN], F32) for i in range(2)]
    sx = sb("cv_sx", [128, TOWN], BF16)
    sgt = sb("cv_sgt", [128, TOWN], BF16)
    h1 = sb("cv_h1", [128, TOWN], BF16)

    lw = [
        p.load("sync", cb[:], ctx.t["conv_b2"][:]),
        p.load("sync", dv[:], ctx.t["Dvec"][:]),
        p.load("sync", gz[:], ctx.t["gzero"][:]),
    ]
    for tau in range(4):
        lw.append(p.load(
            "sync", cwt[tau][:], ctx.t["conv_wT"][:, tau * IC1:(tau + 1) * IC1]))
    done = {}
    last = {}
    h1_done = {}
    sgt_done = {}
    st_done = {}
    SEG = 8 * 131

    def chunk(ic, xw1, xw2):
        bi = ic % 2
        ws = [done[(ic - 2, 1)]] if (ic - 2, 1) in done else []
        lxi = p.load("sync", xi[bi][:], ctx.t["xp"][ic * 128:(ic + 1) * 128, :], waits=ws + xw1)
        lgt = p.load("sync", gt[:], ctx.t["xp"][(IC1 + ic) * 128:(IC1 + ic + 1) * 128, 0:TOWN],
                     waits=[sgt_done[hh] for hh in sgt_done] + xw2)
        for hh in range(2):
            s0 = hh * 8
            a0 = hh * SEG
            c0 = hh * 1024
            g0 = hh * 24
            asmv = asm[:, a0:a0 + SEG].rearrange("p (s u) -> p s u", s=8, u=131)
            xiv = xi[bi][:, c0:c0 + 1024].rearrange("p (s u) -> p s u", s=8, u=128)
            xgv = xi[bi][:, TOWN + g0:TOWN + g0 + 24].rearrange("p (s u) -> p s u", s=8, u=3)
            gzv = gz[:, g0:g0 + 24].rearrange("p (s u) -> p s u", s=8, u=3)

            def assemble(v, asmv=asmv, xiv=xiv, xgv=xgv, gzv=gzv):
                v.tensor_copy(asmv[:, :, 3:131], xiv)
                return v.tensor_tensor(asmv[:, :, 0:3], xgv, gzv, ALU.mult)
            asm_t = p.task("vector", assemble,
                           waits=lw + [lxi] + ([last[hh]] if hh in last else []), serial=False)

            shifted = lambda tau, asmv=asmv: asmv[:, :, tau:tau + 128]
            acv = lambda j, c0=c0: ac[j][:, c0:c0 + 1024].rearrange("p (s u) -> p s u", s=8, u=128)
            t1_t = p.task("vector", lambda v, ic=ic, asmv=asmv, c0=c0: v.tensor_scalar(
                acv(0, c0), shifted(0, asmv), cwt[0][:, ic:ic + 1], cb[:, ic:ic + 1],
                ALU.mult, ALU.add),
                waits=[asm_t] + ([h1_done[hh]] if hh in h1_done else []), serial=False)
            t2_t = p.task("vector", lambda v, ic=ic, asmv=asmv, c0=c0: v.scalar_tensor_tensor(
                acv(1, c0), shifted(1, asmv), cwt[1][:, ic:ic + 1], acv(0, c0), ALU.mult, ALU.add),
                waits=[t1_t], serial=False)
            t3_t = p.task("vector", lambda v, ic=ic, asmv=asmv, c0=c0: v.scalar_tensor_tensor(
                acv(0, c0), shifted(2, asmv), cwt[2][:, ic:ic + 1], acv(1, c0), ALU.mult, ALU.add),
                waits=[t2_t], serial=False)
            t4_t = p.task("vector", lambda v, ic=ic, asmv=asmv, c0=c0: v.scalar_tensor_tensor(
                acv(1, c0), shifted(3, asmv), cwt[3][:, ic:ic + 1], acv(0, c0), ALU.mult, ALU.add),
                waits=[t3_t], serial=False)
            last[hh] = t4_t
            sx_t = p.task("scalar", lambda a, c0=c0: a.activation(
                sx[:, c0:c0 + 1024], ac[1][:, c0:c0 + 1024], AF.Silu),
                waits=[t4_t] + ([h1_done[hh]] if hh in h1_done else []), serial=False)
            sg_t = p.task("scalar", lambda a, c0=c0: a.activation(
                sgt[:, c0:c0 + 1024], gt[:, c0:c0 + 1024], AF.Silu),
                waits=[lgt] + ([h1_done[hh]] if hh in h1_done else []), serial=False)
            sgt_done[hh] = sg_t
            h1_t = p.task("vector", lambda v, ic=ic, c0=c0: v.scalar_tensor_tensor(
                h1[:, c0:c0 + 1024], sx[:, c0:c0 + 1024], dv[:, ic:ic + 1], sgt[:, c0:c0 + 1024],
                ALU.mult, ALU.mult),
                waits=[sx_t, sg_t] + ([st_done[hh]] if hh in st_done else []), serial=False)
            h1_done[hh] = h1_t
            done[(ic, hh)] = p.store("sync",
                                     ctx.t["h1"][ic * 128:(ic + 1) * 128, c0:c0 + 1024],
                                     h1[:, c0:c0 + 1024], waits=[h1_t], serial=False,
                                     slot=f"cv_h1{hh}")
            st_done[hh] = done[(ic, hh)]
    return chunk


def ph_conv(ctx):
    p = Ph(ctx)
    with ExitStack() as ex:
        chunk = build_conv(p, ex, ctx)
        for ic in range(IC1):
            chunk(ic, [], [])
    p.emit()

def ph_wout(ctx):
    nc = ctx.nc
    p = Ph(ctx)
    with ExitStack() as ex:
        sb = lambda n, s, d: ex.enter_context(nc.sbuf_tensor(f"z{ctx.phase_idx}_{n}", s, d))
        psum = lambda n, s: ex.enter_context(nc.psum_tensor(f"z{ctx.phase_idx}_{n}", s, F32))
        hb = sb("wo_h", [128, IC1, 1024], BF16)
        ww = [sb(f"wo_w{i}", [128, IC1, 128], BF16) for i in range(2)]
        xa = [sb(f"wo_xa{i}", [128, 512], F32) for i in range(2)]
        xo = [sb(f"wo_xo{i}", [128, 512], F32) for i in range(2)]
        pso = [psum(f"wo_pso{i}", [128, 512]) for i in range(2)]

        h_consumed = None
        wconsumed = {}
        evac_done = {}
        xa_done = {}
        st_done = {}
        gi = 0
        for si, (s0, sw) in enumerate([(0, 1024), (1024, 1024)]):
            lh = p.load("sync", hb[:, :, 0:sw], feat(ctx.t["h1"])[:, :, s0:s0 + sw],
                        waits=([h_consumed] if h_consumed else []))
            for oc in range(HC):
                bi = oc % 2
                gw_ = si * HC + oc
                ldw = p.load("gpsimd", ww[bi][:], feat(ctx.t["w_out_t"])[:, :, oc * 128:(oc + 1) * 128],
                             waits=([wconsumed[gw_ - 2]] if gw_ - 2 in wconsumed else []))
                for (t0, w) in tblocks(sw):
                    pb = gi % 2
                    lx = p.load("sync", xa[pb][:, 0:w], feat(ctx.t["x1"])[:, oc, s0 + t0:s0 + t0 + w],
                                waits=([xa_done[pb]] if pb in xa_done else []))

                    def mm(t, bi=bi, t0=t0, w=w, pb=pb):
                        for c in range(IC1):
                            r = t.matmul(pso[pb][:, 0:w], ww[bi][:, c, :], hb[:, c, t0:t0 + w],
                                         start=(c == 0), stop=(c == IC1 - 1), skip_group_check=True)
                        return r
                    mm_t = p.task("tensor", mm, waits=[lh, ldw] + ([evac_done[pb]] if pb in evac_done else []),
                                  serial=False)
                    ev_t = p.task("vector", lambda v, w=w, pb=pb: v.tensor_tensor(
                        xo[pb][:, 0:w], pso[pb][:, 0:w], xa[pb][:, 0:w], ALU.add),
                        waits=[mm_t, lx] + ([st_done[pb]] if pb in st_done else []), serial=False)
                    evac_done[pb] = ev_t
                    xa_done[pb] = ev_t
                    st_done[pb] = p.store("sync", feat(ctx.t["x2a"])[:, oc, s0 + t0:s0 + t0 + w],
                                          xo[pb][:, 0:w], waits=[ev_t], serial=False)
                    gi += 1
                wconsumed[gw_] = mm_t
            h_consumed = mm_t
    p.emit()


def build_quant_thunks(p, ex, ctx):
    nc = ctx.nc
    sb = lambda n, s, d: ex.enter_context(nc.sbuf_tensor(f"z{ctx.phase_idx}_{n}", s, d))
    psum = lambda n, s: ex.enter_context(nc.psum_tensor(f"z{ctx.phase_idx}_{n}", s, F32))
    HB = 1408
    wf = [sb(f"qz_wf{i}", [128, HB], F32) for i in range(4)]
    a1b = [sb(f"qz_a1{i}", [128, HB], BF16) for i in range(2)]
    qn = [sb(f"qz_qn{i}", [128, HB], BF16) for i in range(2)]
    acc = sb("qz_acc", [128, 4], F32)
    part = sb("qz_part", [128, 4], F32)
    onesc = sb("qz_ones", [128, 1], F32)
    onesr = sb("qz_onesr", [1, 128], F32)
    thr = sb("qz_thr", [1, 1], F32)
    thp = sb("qz_thp", [128, 1], F32)
    thn = sb("qz_thn", [128, 1], F32)
    qzps = psum("qz_ps", [128, 8])
    pss4 = qzps[0:1, 2:6]
    psb = qzps[:, 1:2]
    ssum = sb("qz_ssum", [1, 1], F32)

    st = {"red": {}, "acc": {}, "t4": None, "qd": {}, "wfd": {}, "od": None}
    thunks = []

    def t_init():
        st["od"] = p.task("gpsimd", lambda g: (g.memset(onesc[:], 1.0), g.memset(onesr[:], 1.0))[-1],
                          serial=False)
    thunks.append(t_init)

    specs = [("w1_t", "w1qn", D, INTER), ("w2_t", "w2qn", D, INTER), ("w3_t", "w3qn", INTER, D)]
    for (srcn, dstn, rows, cols) in specs:
        nch = rows // 128

        def t_zero(srcn=srcn):
            z = p.task("vector", lambda v: v.memset(acc[:], 0.0),
                       waits=([st["sm"]] if st.get("sm") else []) + list(st["acc"].values()),
                       serial=False)
            st["acc"] = {b: z for b in range(4)}
        thunks.append(t_zero)

        AQW = 1408
        idx = [0]
        for c in range(nch):
            for h0 in range(0, cols, AQW):
                def t_ac(c=c, h0=h0, hw=min(AQW, cols - h0), srcn=srcn):
                    w_t = feat(ctx.t[srcn])
                    bb = idx[0] % 4
                    idx[0] += 1
                    awaits = [st["red"][bb]] if bb in st["red"] else []
                    if bb in st["wfd"]:
                        awaits.append(st["wfd"][bb])
                    lw_ = p.load("sync", wf[bb][:, 0:hw], w_t[:, c, h0:h0 + hw], waits=awaits)
                    r_t = p.task("vector", lambda v, bb=bb, hw=hw: v.reduce_sum(
                        part[:, bb:bb + 1], wf[bb][:, 0:hw], axis=AX.X, apply_absolute_value=True),
                        waits=[lw_] + ([st["acc"][bb]] if bb in st["acc"] else []), serial=False)
                    st["red"][bb] = r_t
                    st["acc"][bb] = p.task("vector", lambda v, bb=bb: v.tensor_tensor(
                        acc[:, bb:bb + 1], acc[:, bb:bb + 1], part[:, bb:bb + 1], ALU.add),
                        waits=[r_t], serial=False)
                thunks.append(t_ac)

        def t_gamma():
            sm = p.task("tensor", lambda t: t.matmul(pss4, onesc[:], acc[:], start=True, stop=True,
                                                     skip_group_check=True),
                        waits=list(st["acc"].values()) + [st["od"]] +
                        ([st["t4"]] if st["t4"] else []), serial=False)
            st["sm"] = sm
            sr = p.task("vector", lambda v: v.reduce_sum(ssum[:], pss4, axis=AX.X),
                        waits=[sm], serial=False)
            t1 = p.task("scalar", lambda a: a.activation(thr[:], ssum[:], AF.Copy, scale=1.0 / QNW,
                                                         bias=1e-8), waits=[sr], serial=False)
            t2 = p.task("scalar", lambda a: a.mul(thr[:], thr[:], 0.5), waits=[t1], serial=False)
            bc = p.task("tensor", lambda t: t.matmul(psb, onesr[:], thr[:], start=True, stop=True,
                                                     skip_group_check=True), waits=[t2], serial=False)
            t3 = p.task("scalar", lambda a: a.activation(thp[:], psb, AF.Copy), waits=[bc], serial=False)
            st["t4"] = p.task("scalar", lambda a: a.activation(thn[:], psb, AF.Copy, scale=-1.0),
                              waits=[t3], serial=False)
        thunks.append(t_gamma)

        QW = 1408
        halves = [(h0, min(QW, cols - h0)) for h0 in range(0, cols, QW)]
        for c in range(nch):
            for (h0, hw) in halves:
                def t_b(c=c, h0=h0, hw=hw, srcn=srcn, dstn=dstn):
                    w_t = feat(ctx.t[srcn])
                    dst_t = feat(ctx.t[dstn])
                    bb = idx[0] % 4
                    ob = idx[0] % 2
                    idx[0] += 1
                    bwaits = [st["wfd"][bb]] if bb in st["wfd"] else []
                    if bb in st["red"]:
                        bwaits.append(st["red"][bb])
                    lw_ = p.load("sync", wf[bb][:, 0:hw], w_t[:, c, h0:h0 + hw], waits=bwaits)
                    g_t = p.task("vector", lambda v, bb=bb, hw=hw, ob=ob: v.tensor_scalar(
                        a1b[ob][:, 0:hw], wf[bb][:, 0:hw], thp[:], 1.0, ALU.is_gt, ALU.mult),
                        waits=[lw_, st["t4"]] + ([st["qd"][ob]] if ob in st["qd"] else []), serial=False)
                    s_t = p.task("vector", lambda v, bb=bb, hw=hw, ob=ob: v.scalar_tensor_tensor(
                        qn[ob][:, 0:hw], wf[bb][:, 0:hw], thn[:], a1b[ob][:, 0:hw], ALU.is_lt,
                        ALU.subtract), waits=[g_t], serial=False)
                    st["wfd"][bb] = s_t
                    st["qd"][ob] = p.store("gpsimd", dst_t[:, c, h0:h0 + hw], qn[ob][:, 0:hw],
                                           waits=[s_t], serial=False)
                thunks.append(t_b)
    return thunks


def ph_quant(ctx):
    p = Ph(ctx)
    with ExitStack() as ex:
        for t in build_quant_thunks(p, ex, ctx):
            t()
    p.emit()


def ph_attnq(ctx):
    p = Ph(ctx)
    with ExitStack() as ex:
        thunks = build_quant_thunks(p, ex, ctx)
        stt = {"i": 0}
        step = max(1, (len(thunks) + 71) // 72)

        def feeder():
            tgt = min(len(thunks), stt["i"] + step)
            while stt["i"] < tgt:
                thunks[stt["i"]]()
                stt["i"] += 1
        emit_attn(p, ex, ctx, feeder=feeder)
        while stt["i"] < len(thunks):
            thunks[stt["i"]]()
            stt["i"] += 1
    p.emit()


def ph_lwc(ctx):
    p = Ph(ctx)
    with ExitStack() as ex:
        lnd = emit_ln(p, ex, ctx, "x1", "xn1", TEXT, tag="ln1", lean=True)
        chunk = build_conv(p, ex, ctx)

        def pair_cb(c, d_xi, d_gt):
            chunk(c, [d_xi], [d_gt])
        emit_win(p, ex, ctx, ln_done=lnd, pair_cb=pair_cb)
    p.emit()


def ph_ffn1a(ctx):
    nc = ctx.nc
    p = Ph(ctx)
    with ExitStack() as ex:
        sb = lambda n, s, d: ex.enter_context(nc.sbuf_tensor(f"z{ctx.phase_idx}_{n}", s, d))
        psum = lambda n, s: ex.enter_context(nc.psum_tensor(f"z{ctx.phase_idx}_{n}", s, F32))
        xn = sb("f1_xn", [128, HC, TOWN], BF16)
        w1 = [sb(f"f1_w1{i}", [128, HC, 128], BF16) for i in range(2)]
        w2 = [sb(f"f1_w2{i}", [128, HC, 128], BF16) for i in range(2)]
        sg = [sb(f"f1_sg{i}", [128, 512], BF16) for i in range(2)]
        hsb = [sb(f"f1_h{i}", [128, TOWN], BF16) for i in range(2)]
        ps1 = [psum(f"f1_ps1{i}", [128, 512]) for i in range(2)]
        ps2 = [psum(f"f1_ps2{i}", [128, 512]) for i in range(2)]

        lxn = p.load("sync", xn[:], feat(ctx.t["xn1"])[:, :, 0:TOWN])
        done = {}
        wconsumed = {}
        sig_done = {}
        mul_done = {}
        for ic in range(IC0):
            bi = ic % 2
            ws = [wconsumed[ic - 2]] if ic - 2 in wconsumed else []
            l1 = p.load("sync", w1[bi][:], feat(ctx.t["w1qn"])[:, :, ic * 128:(ic + 1) * 128], waits=ws)
            l2 = p.load("sync", w2[bi][:], feat(ctx.t["w2qn"])[:, :, ic * 128:(ic + 1) * 128], waits=ws)
            muls = []
            for tbi, (t0, w) in enumerate(TB_OWN):
                pb = tbi % 2

                def mm1(t, bi=bi, t0=t0, w=w, pb=pb):
                    for c in range(HC):
                        r = t.matmul(ps1[pb][:, 0:w], w1[bi][:, c, :], xn[:, c, t0:t0 + w],
                                     start=(c == 0), stop=(c == HC - 1), skip_group_check=True)
                    return r
                m1_t = p.task("tensor", mm1, waits=[lxn, l1] + ([sig_done[pb]] if pb in sig_done else []),
                              serial=False)
                sg_t = p.task("scalar", lambda a, w=w, pb=pb: a.activation(
                    sg[pb][:, 0:w], ps1[pb][:, 0:w], AF.Sigmoid, scale=-1.0),
                    waits=[m1_t] + ([mul_done[pb]] if pb in mul_done else []), serial=False)
                sig_done[pb] = sg_t

                def mm2(t, bi=bi, t0=t0, w=w, pb=pb):
                    for c in range(HC):
                        r = t.matmul(ps2[pb][:, 0:w], w2[bi][:, c, :], xn[:, c, t0:t0 + w],
                                     start=(c == 0), stop=(c == HC - 1), skip_group_check=True)
                    return r
                m2_t = p.task("tensor", mm2, waits=[l2] + ([mul_done[pb]] if pb in mul_done else []),
                              serial=False)
                tw_ = [m2_t, sg_t] + ([done[ic - 2]] if ic - 2 in done and tbi == 0 else [])
                ml_t = p.task("vector", lambda v, bi=bi, t0=t0, w=w, pb=pb: v.tensor_tensor(
                    hsb[bi][:, t0:t0 + w], ps2[pb][:, 0:w], sg[pb][:, 0:w], ALU.mult),
                    waits=tw_, serial=False)
                mul_done[pb] = ml_t
                muls.append(ml_t)
            wconsumed[ic] = muls[-1]
            done[ic] = p.store("scalar", ctx.t["hq"][ic * 128:(ic + 1) * 128, :], hsb[bi][:],
                               waits=muls, serial=False)
    p.emit()


def ph_ffn1b(ctx):
    nc = ctx.nc
    p = Ph(ctx)
    with ExitStack() as ex:
        sb = lambda n, s, d: ex.enter_context(nc.sbuf_tensor(f"z{ctx.phase_idx}_{n}", s, d))
        psum = lambda n, s: ex.enter_context(nc.psum_tensor(f"z{ctx.phase_idx}_{n}", s, F32))
        hb = sb("f1b_h", [128, IC0, 1024], BF16)
        ww = [sb(f"f1b_w{i}", [128, IC0, 128], BF16) for i in range(2)]
        xa = [sb(f"f1b_xa{i}", [128, 512], F32) for i in range(2)]
        xo = [sb(f"f1b_xo{i}", [128, 512], F32) for i in range(2)]
        pso = [psum(f"f1b_pso{i}", [128, 512]) for i in range(2)]

        h_consumed = None
        wconsumed = {}
        evac_done = {}
        xa_done = {}
        st_done = {}
        gi = 0
        for si, (s0, sw) in enumerate([(0, 1024), (1024, 1024)]):
            lh = p.load("sync", hb[:, :, 0:sw], feat(ctx.t["hq"])[:, :, s0:s0 + sw],
                        waits=([h_consumed] if h_consumed else []))
            for oc in range(HC):
                bi = oc % 2
                gw_ = si * HC + oc
                ldw = p.load("sync", ww[bi][:], feat(ctx.t["w3qn"])[:, :, oc * 128:(oc + 1) * 128],
                             waits=([wconsumed[gw_ - 2]] if gw_ - 2 in wconsumed else []))
                for (t0, w) in tblocks(sw):
                    pb = gi % 2
                    lx = p.load("sync", xa[pb][:, 0:w], feat(ctx.t["x2a"])[:, oc, s0 + t0:s0 + t0 + w],
                                waits=([xa_done[pb]] if pb in xa_done else []))

                    def mm(t, bi=bi, t0=t0, w=w, pb=pb):
                        for c in range(IC0):
                            r = t.matmul(pso[pb][:, 0:w], ww[bi][:, c, :], hb[:, c, t0:t0 + w],
                                         start=(c == 0), stop=(c == IC0 - 1), skip_group_check=True)
                        return r
                    mm_t = p.task("tensor", mm, waits=[lh, ldw] + ([evac_done[pb]] if pb in evac_done else []),
                                  serial=False)
                    ev_t = p.task("vector", lambda v, w=w, pb=pb: v.tensor_tensor(
                        xo[pb][:, 0:w], pso[pb][:, 0:w], xa[pb][:, 0:w], ALU.add),
                        waits=[mm_t, lx] + ([st_done[pb]] if pb in st_done else []), serial=False)
                    evac_done[pb] = ev_t
                    xa_done[pb] = ev_t
                    st_done[pb] = p.store("sync", feat(ctx.t["out"])[:, oc, s0 + t0:s0 + t0 + w],
                                          xo[pb][:, 0:w], waits=[ev_t], serial=False)
                    gi += 1
                wconsumed[gw_] = mm_t
            h_consumed = mm_t
    p.emit()


def ph_lnkv(ctx):
    p = Ph(ctx)
    with ExitStack() as ex:
        xnb = [ex.enter_context(ctx.nc.sbuf_tensor(f"z{ctx.phase_idx}_sh_xnb{i}",
                                                   [128, HC, 512], BF16)) for i in range(2)]
        p.cur_group = "kv"
        kv_iter, kv_fin, xn_used = emit_kv(p, ex, ctx, defer=True, xnb_ext=xnb)

        pend = []

        def cb(i, h):
            pend.append((i, h))
            if len(pend) > 1:
                j, hj = pend.pop(0)
                p.cur_group = "kv"
                kv_iter(j, hj)
                p.cur_group = "ln"
        p.cur_group = "ln"
        emit_ln(p, ex, ctx, "x_kv", "xn_kv", S, lean=True, iter_cb=cb,
                xn_share=xnb, reuse_wait=lambda j: xn_used.get(j))
        p.cur_group = "kv"
        for j, hj in pend:
            kv_iter(j, hj)
        kv_fin()
    p.emit()


def ph_lnq(ctx):
    p = Ph(ctx)
    with ExitStack() as ex:
        xnb = [ex.enter_context(ctx.nc.sbuf_tensor(f"z{ctx.phase_idx}_sh_xnb{i}",
                                                   [128, HC, 512], BF16)) for i in range(2)]
        p.cur_group = "q"
        q_iter, q_fin, xn_used = emit_q(p, ex, ctx, defer=True, xnb_ext=xnb)

        pend = []

        def cb(i, h):
            pend.append((i, h))
            if len(pend) > 1:
                j, hj = pend.pop(0)
                p.cur_group = "q"
                q_iter(j, hj)
                p.cur_group = "ln"
        p.cur_group = "ln"
        emit_ln(p, ex, ctx, "x_own", "xn_own", TEXT, lean=True, iter_cb=cb,
                xn_share=xnb, reuse_wait=lambda j: xn_used.get(j))
        p.cur_group = "q"
        for j, hj in pend:
            q_iter(j, hj)
        q_fin()
    p.emit()


PHASES = [
    ("lnkv", ph_lnkv, ["x_kv", "wk_t", "kA_t", "kB_t", "wv_t", "vA_t", "vB_t", "tabk1", "tabk2"],
     ["xn_kv", "k_rot", "v_tok", "kmax2"]),
    ("lnq", ph_lnq, ["x_own", "wq_t", "qA_t", "qB_t", "tabq1", "tabq2", "kmax2"],
     ["xn_own", "q_rot", "nbias"]),
    ("attnq", ph_attnq,
     ["k_rot", "v_tok", "q_rot", "nbias", "maskblk", "gmask", "w1_t", "w2_t", "w3_t"],
     ["ao", "w1qn", "w2qn", "w3qn"]),
    ("oproj", ph_oproj, ["ao", "wo_t", "oA_t", "oB_t", "x_own"], ["x1a"]),
    ("ffn0a", ph_ffn0a, ["xn_own", "gate_t", "up_t", "gate_s", "up_s"], ["h0"]),
    ("ffn0b", ph_ffn0b, ["h0", "down_t", "down_s", "x1a"], ["x1"]),
    ("lwc", ph_lwc, ["x1", "w_in_t", "conv_wT", "conv_b2", "Dvec", "gzero"],
     ["xn1", "xp", "h1"]),
    ("wout", ph_wout, ["h1", "w_out_t", "x1"], ["x2a"]),
    ("ffn1a", ph_ffn1a, ["xn1", "w1qn", "w2qn"], ["hq"]),
    ("ffn1b", ph_ffn1b, ["hq", "w3qn", "x2a"], ["out"]),
]


def _decl(nc, name, kind):
    shape, dt = STAGED.get(name) or SCRATCH[name]
    return nc.dram_tensor(name, shape, dt, kind=kind).ap()


def build_phase_program(idx):
    name, fn, ins, outs = PHASES[idx]
    nc = bass.Bass("TRN2", target_bir_lowering=False, debug=False)
    aps = {}
    for n in ins:
        aps[n] = _decl(nc, n, "ExternalInput")
    for n in outs:
        aps[n] = _decl(nc, n, "ExternalOutput")
    with ExitStack() as ex:
        sems = {n: ex.enter_context(nc.semaphore(n)) for n in sem_names()}
        blk = ex.enter_context(nc.Block())
        ctx = Ctx(nc, blk, aps, sems, {n: 0 for n in sem_names()}, phase_idx=0)
        fn(ctx)
    return nc


def build_fused_program(phase_subset=None):
    nc = bass.Bass("TRN2", target_bir_lowering=False, debug=False)
    aps = {}
    for n in STAGED:
        aps[n] = _decl(nc, n, "ExternalInput")
    for n in SCRATCH:
        aps[n] = _decl(nc, n, "ExternalOutput" if n == "out" else "Internal")
    with ExitStack() as ex:
        sems = {n: ex.enter_context(nc.semaphore(n)) for n in sem_names()}
        blk = ex.enter_context(nc.Block())
        ctx = Ctx(nc, blk, aps, sems, {n: 0 for n in sem_names()}, phase_idx=0)
        for i, (name, fn, ins, outs) in enumerate(PHASES):
            if phase_subset is not None and name not in phase_subset:
                continue
            fn(ctx)
    return nc


def _perm(nheads):
    idx = []
    for h in range(nheads):
        idx.extend(h * HD + 2 * np.arange(64))
        idx.extend(h * HD + 2 * np.arange(64) + 1)
    return np.array(idx)


def stage_all(inputs):
    f32 = np.float32
    gp = {}

    pq = _perm(NQH)
    pk = _perm(NKV)
    gp["wq_t"] = np.ascontiguousarray(inputs["L0_attn_Wq"][pq, :].T.astype(f32))
    gp["qA_t"] = np.ascontiguousarray(inputs["L0_attn_qA"].T.astype(f32))
    gp["qB_t"] = np.ascontiguousarray(inputs["L0_attn_qB"][pq, :].T.astype(f32))
    gp["wk_t"] = np.ascontiguousarray(inputs["L0_attn_Wk"][pk, :].T.astype(f32))
    gp["kA_t"] = np.ascontiguousarray(inputs["L0_attn_kA"].T.astype(f32))
    gp["kB_t"] = np.ascontiguousarray(inputs["L0_attn_kB"][pk, :].T.astype(f32))
    gp["wv_t"] = np.ascontiguousarray(inputs["L0_attn_Wv"].T.astype(f32))
    gp["vA_t"] = np.ascontiguousarray(inputs["L0_attn_vA"].T.astype(f32))
    gp["vB_t"] = np.ascontiguousarray(inputs["L0_attn_vB"].T.astype(f32))
    gp["wo_t"] = np.ascontiguousarray(inputs["L0_attn_Wo"].T.astype(f32))
    gp["oA_t"] = np.ascontiguousarray(inputs["L0_attn_oA"].T.astype(f32))
    gp["oB_t"] = np.ascontiguousarray(inputs["L0_attn_oB"].T.astype(f32))
    gp["gate_t"] = np.ascontiguousarray(inputs["L0_ffn_gate_w"].T.astype(f32))
    gp["up_t"] = np.ascontiguousarray(inputs["L0_ffn_up_w"].T.astype(f32))
    gp["down_t"] = np.ascontiguousarray(inputs["L0_ffn_down_w"].T.astype(f32))
    gp["gate_s"] = np.ascontiguousarray(np.asarray(inputs["L0_ffn_gate_s"], f32).reshape(IC0, 128).T)
    gp["up_s"] = np.ascontiguousarray(np.asarray(inputs["L0_ffn_up_s"], f32).reshape(IC0, 128).T)
    gp["down_s"] = np.ascontiguousarray(np.asarray(inputs["L0_ffn_down_s"], f32).reshape(HC, 128).T)
    gp["w_in_t"] = np.ascontiguousarray(inputs["L1_attn_w_in"].T.astype(f32))
    gp["w_out_t"] = np.ascontiguousarray(inputs["L1_attn_w_out"].T.astype(f32))
    cw_ = np.asarray(inputs["L1_attn_conv_w"], f32)[:, 0, :]
    gp["conv_wT"] = np.ascontiguousarray(np.concatenate(
        [cw_[:, t].reshape(IC1, 128).T for t in range(4)], axis=1))
    gp["conv_b2"] = np.ascontiguousarray(np.asarray(inputs["L1_attn_conv_b"], f32).reshape(IC1, 128).T)
    gp["Dvec"] = np.ascontiguousarray(np.asarray(inputs["L1_attn_D"], f32).reshape(IC1, 128).T)
    gp["w1_t"] = np.ascontiguousarray(inputs["L1_ffn_w1"].T.astype(f32))
    gp["w2_t"] = np.ascontiguousarray(inputs["L1_ffn_w2"].T.astype(f32))
    gp["w3_t"] = np.ascontiguousarray(inputs["L1_ffn_w3"].T.astype(f32))

    invf = 1.0 / (10000.0 ** (np.arange(0, HD, 2, dtype=np.float64) / HD))
    x = np.asarray(inputs["L0_attn_x"], dtype=f32)

    in_maps = []
    for c in range(NCORE):
        b, j = c // 2, c % 2
        m = dict(gp)
        own_pos = np.concatenate([np.arange((2 * s + j) * 128, (2 * s + j) * 128 + 128)
                                  for s in range(16)])
        gpos = np.concatenate([np.arange((2 * s + j) * 128 - 3, (2 * s + j) * 128)
                               for s in range(16)])
        gvalid = gpos >= 0
        gpos_c = np.where(gvalid, gpos, 0)
        xb = x[b]
        xo = np.empty((D, TEXT), f32)
        xo[:, :TOWN] = xb[own_pos, :].T
        xg = xb[gpos_c, :].T.copy()
        xg[:, ~gvalid] = 0.0
        xo[:, TOWN:] = xg
        m["x_own"] = np.ascontiguousarray(xo)
        m["x_kv"] = np.ascontiguousarray(xb.T)

        qpos = np.concatenate([own_pos, gpos_c]).astype(np.float64)
        fr = qpos[None, :] * invf[:, None]
        t1 = np.empty((128, TEXT), f32)
        t1[0:64] = np.cos(fr)
        t1[64:128] = np.cos(fr)
        t2 = np.empty((128, TEXT), f32)
        t2[0:64] = -np.sin(fr)
        t2[64:128] = np.sin(fr)
        m["tabq1"] = np.ascontiguousarray(t1)
        m["tabq2"] = np.ascontiguousarray(t2)
        frk = np.arange(S, dtype=np.float64)[None, :] * invf[:, None]
        t1 = np.empty((128, S), f32)
        t1[0:64] = np.cos(frk)
        t1[64:128] = np.cos(frk)
        t2 = np.empty((128, S), f32)
        t2[0:64] = -np.sin(frk)
        t2[64:128] = np.sin(frk)
        m["tabk1"] = np.ascontiguousarray(t1)
        m["tabk2"] = np.ascontiguousarray(t2)

        kr = np.arange(128)
        mb = np.zeros((128, 8, 4, 512), np.float32)
        for qb in range(8):
            for mm_ in range(4):
                kpos = (4 * qb + mm_) * 128 + kr
                for half in range(2):
                    for wtile in range(2):
                        i = 2 * qb + wtile
                        g = 2 * i + j
                        qpos_t = g * 128 + np.arange(128)
                        valid = kpos[:, None] <= qpos_t[None, :]
                        mb[:, qb, mm_, half * 256 + wtile * 128: half * 256 + (wtile + 1) * 128] = valid
        m["maskblk"] = np.ascontiguousarray(mb.reshape(128, -1).astype(BFD))

        gm = np.zeros((128, 32, 2, 48), np.float32)
        for kt in range(32):
            kpos = kt * 128 + kr
            val = (kpos[:, None] <= gpos[None, :]).astype(np.float32)
            inv = ~gvalid
            if inv.any() and kt == 0:
                val[0, inv] = 1.0
            gm[:, kt, 0, :] = val
            gm[:, kt, 1, :] = val
        m["gmask"] = np.ascontiguousarray(gm.reshape(128, -1).astype(BFD))

        gz = np.ones((128, NG), np.float32)
        gz[:, ~gvalid] = 0.0
        m["gzero"] = np.ascontiguousarray(gz.astype(BFD))
        in_maps.append(m)
    return in_maps


def assemble(outs):
    res = np.empty((B, S, D), np.float32)
    for c in range(NCORE):
        b, j = c // 2, c % 2
        o = np.asarray(outs[c])
        for s in range(16):
            g = 2 * s + j
            res[b, g * 128:(g + 1) * 128, :] = o[:, s * 128:(s + 1) * 128].T
    return res


def run_unfused(in_maps, upto=None, only=None):
    state = [dict(m) for m in in_maps]
    for idx, (name, fn, ins, outs) in enumerate(PHASES):
        if upto is not None and idx > upto:
            break
        if only is not None and name not in only:
            continue
        nc = build_phase_program(idx)
        maps = [{n: st[n] for n in ins} for st in state]
        res = run_bass_kernel_spmd(nc, maps, core_ids=list(range(NCORE)))
        for c in range(NCORE):
            for n in outs:
                state[c][n] = np.asarray(res.results[c][n])
    return state


def kernel(**inputs):
    in_maps = stage_all(inputs)
    if FUSED:
        nc = build_fused_program()
        res = run_bass_kernel_spmd(nc, in_maps, core_ids=list(range(NCORE)))
        return assemble([r["out"] for r in res.results])
    state = run_unfused(in_maps)
    return assemble([st["out"] for st in state])
